# version 19 (final)
# speedup vs baseline: 1.0722x; 1.0008x over previous
import sys

sys.path.insert(0, "/opt/trn_rl_repo")
import numpy as np
import concourse.bass as bass
import concourse.mybir as mybir
from concourse.bass_utils import run_bass_kernel_spmd
from bass_rust import AP as RawAP

F32 = mybir.dt.float32
BF16 = mybir.dt.bfloat16
I32 = mybir.dt.int32
ALU = mybir.AluOpType
ACT = mybir.ActivationFunctionType
AXX = mybir.AxisListType.X

D = 1024
L = 4096
B = 8
HID = 4096
DC = 8
HC = 32
TT = 512
NTT = L // TT
KB = 8
HPL = 4352
LNEPS = 1e-5


def cdiv(a, b):
    return (a + b - 1) // b


class Sem:
    __slots__ = ("h", "n")

    def __init__(self, h):
        self.h = h
        self.n = 0


class Prog:
    ENGINES = ("sync", "vector", "scalar", "tensor", "gpsimd")

    def __init__(self, nc, block):
        self.nc, self.block = nc, block
        self.q = {e: [] for e in self.ENGINES}

    def op(self, eng, fn, sem=None, k=1):
        if sem is not None:
            sem.n += k
            self.q[eng].append(lambda e, fn=fn, h=sem.h, k=k: fn(e).then_inc(h, k))
        else:
            self.q[eng].append(lambda e, fn=fn: fn(e))

    def wait(self, eng, sem, val=None):
        v = sem.n if val is None else val
        self.q[eng].append(lambda e, h=sem.h, v=v: e.wait_ge(h, v))

    def barrier(self, sem, engines=ENGINES):
        for e in engines:
            sem.n += 1
            self.q[e].append(lambda e_, h=sem.h: e_.nop().then_inc(h, 1))
        for e in engines:
            self.wait(e, sem)

    def flush(self):
        for name in self.ENGINES:
            qq = self.q[name]
            if not qq:
                continue

            def body(e, qq=qq):
                for f in qq:
                    f(e)
            getattr(self.block, name)(body)
        self.q = {e: [] for e in self.ENGINES}


def rap(base_ap, offset_elems, dims):
    return RawAP(base_ap.tensor, offset_elems, [list(d) for d in dims])


class SemBank:
    NAMES = ["ld", "ld2", "mm", "ev", "evs", "gp", "st", "cc", "ph", "tp",
             "sc", "aux", "ldx", "fin", "ldw", "stv", "vx"]

    def __init__(self, nc):
        self.nc = nc
        self.cms = []

    def __enter__(self):
        for n in self.NAMES:
            cm = self.nc.semaphore(n)
            h = cm.__enter__()
            self.cms.append(cm)
            setattr(self, n, Sem(h))
        return self

    def __exit__(self, *a):
        for cm in reversed(self.cms):
            cm.__exit__(*a)
        return False


def build(dumps=()):
    nc = bass.Bass("TRN2", target_bir_lowering=False)
    dd = {}

    def din(name, shape, dtype=F32):
        dd[name] = nc.dram_tensor(name, list(shape), dtype, kind="ExternalInput").ap()

    def dout(name, shape, dtype=F32):
        dd[name] = nc.dram_tensor(name, list(shape), dtype, kind="ExternalOutput").ap()

    def dint(name, shape, dtype=F32, shared=False):
        if name in dumps:
            dd[name] = nc.dram_tensor(name, list(shape), dtype, kind="ExternalOutput").ap()
        else:
            kw = dict(kind="Internal")
            if shared:
                kw["addr_space"] = "Shared"
            dd[name] = nc.dram_tensor(name, list(shape), dtype, **kw).ap()

    din("xfull", [D, B * L])
    din("xmine", [D, L])
    din("w0in_s", [384, D])
    din("w0out", [D, D])
    din("f_w1", [2, 64])
    din("f_b1T", [64, 2])
    din("f_w2s", [128, 128])
    din("f_b2s", [128, 2])
    for l in range(2):
        din(f"ffn{l}_w1", [HID, D])
        din(f"ffn{l}_cw", [128, HC * 3])
        din(f"ffn{l}_cb", [128, HC])
        din(f"ffn{l}_w2", [D, HID])
    din("w1in", [D, 2 * D])
    din("b_in", [128, 2 * DC])
    din("cw4", [128, DC * 4])
    din("cb4", [128, DC])
    din("A_sc", [128, 256])
    din("wB_sc", [128, 256])
    din("bB_sc", [128, 256])
    din("wC_sc", [128, 256])
    din("bC_sc", [128, 256])
    din("wdt", [D, D])
    din("b_dt", [128, DC])
    din("w1outw", [D, D])
    din("b_out", [128, DC])
    din("ln_g", [128, DC])
    din("ln_b", [128, DC])
    dout("out", [D, L])

    dint("sA", [D, L])
    dint("sB", [D, L])
    dint("sC", [D, L])
    dint("zgate", [D, L], BF16)
    dint("ybuf", [D, L])
    dint("projslab", [384, B * L], BF16)
    dint("hpad", [2, 128, HPL], BF16)
    dint("w0T_n", [384, D], BF16)
    dint("w0out_b", [D, D], BF16)
    for l in range(2):
        dint(f"w1q_n{l}", [HID, D], BF16)
        dint(f"w2q_n{l}", [D, HID], BF16)
        dint(f"w1qT{l}", [D, HID], BF16)
        dint(f"w2qT{l}", [HID, D], BF16)
    dint("w1in_b", [D, 2 * D], BF16)
    dint("wdt_b", [D, D], BF16)
    dint("w1out_b", [D, D], BF16)
    dd["_dumps"] = dumps
    dint_always = lambda n, sh, dt_: dd.__setitem__(n, nc.dram_tensor(n, sh, dt_, kind="Internal").ap())
    dint_always("vg_in", [128, 256, 128], BF16)
    dd["vg_out"] = nc.dram_tensor("vg_out", [8, 128, 256, 128], BF16, kind="Internal", addr_space="Shared").ap()
    if "vg_in" in dumps:
        dd["vg_in_dbg"] = nc.dram_tensor("vg_in_dbg", [128, 256, 128], BF16, kind="ExternalOutput").ap()
    if "a3in" in dumps:
        dd["a3in"] = nc.dram_tensor("a3in", [128, DC * 4096], BF16, kind="ExternalOutput").ap()
    if "preA" in dumps:
        dd["preA"] = nc.dram_tensor("preA", [D, L], F32, kind="ExternalOutput").ap()
    if "xpd" in dumps:
        dd["xpd"] = nc.dram_tensor("xpd", [D, L], F32, kind="ExternalOutput").ap()
    for nm in ("xsd", "dtd"):
        if nm in dumps:
            dd[nm] = nc.dram_tensor(nm, [D, L], F32, kind="ExternalOutput").ap()
    if "scn0" in dumps:
        dd["scn0"] = nc.dram_tensor("scn0", [128, 4 * 2048], F32, kind="ExternalOutput").ap()
    if "lnr2" in dumps:
        dd["lnr2"] = nc.dram_tensor("lnr2", [128, 32], F32, kind="ExternalOutput").ap()
    if "lnrow" in dumps:
        dd["lnrow"] = nc.dram_tensor("lnrow", [128, 8], F32, kind="ExternalOutput").ap()

    with (
        nc.sbuf_tensor("cst", [128, 1920], F32) as cst,
        nc.sbuf_tensor("ident", [128, 128], F32) as ident,
        nc.sbuf_tensor("identb", [128, 128], BF16) as identb,
        nc.sbuf_tensor("smls", [128, 2304], F32) as smls,
        SemBank(nc) as S,
        nc.Block() as block,
    ):
        T = dict(cst=cst, ident=ident, identb=identb, smls=smls)
        K_(nc, block, dd, S, T).run()
    return nc


class K_:
    def __init__(self, nc, block, dd, S, T):
        self.nc, self.block, self.dd, self.S, self.T = nc, block, dd, S, T
        self.P = Prog(nc, block)
        self.C = dict(A=0, wB=256, bB=512, wC=768, bC=1024, cw4=1280, cb4=1312,
                      b_dt=1320, b_out=1328, ln_g=1336, ln_b=1344, b_in=1352,
                      ffn0_cw=1368, ffn0_cb=1464, ffn1_cw=1496, ffn1_cb=1592,
                      sw=1624, fb2=1628, ones=1632, oneN=1633, fb1=1634, fw1=1636,
                      onesr=1764)

    def vop(self, fn):
        P, S = self.P, self.S
        P.op("vector", fn, S.vx, 1)
        P.wait("vector", S.vx)

    def run(self):
        nc, P, S = self.nc, self.P, self.S
        with (
            nc.sbuf_tensor("pA", [128, 12544], F32) as pA,
            nc.sbuf_tensor("pB", [128, 16896], F32) as pB,
            nc.sbuf_tensor("pU", [128, 8192], F32) as pU,
            self.psums(8) as ps,
        ):
            self.T.update(pA=pA, pB=pB, pU=pU, ps=ps)
            self.prologue()
            P.barrier(S.ph)
            P.flush()
        with (
            nc.sbuf_tensor("w0T", [128, DC * 384], BF16) as w0T,
            nc.sbuf_tensor("slabT", [128, 49152], BF16) as slabT,
            nc.sbuf_tensor("xb", [128, 2 * DC * 512], BF16) as xbf,
            nc.sbuf_tensor("xf", [128, 2 * 512], F32) as xf32,
            nc.sbuf_tensor("stg", [128, 3 * 512], BF16) as stg,
            nc.sbuf_tensor("eb", [128, 2 * 8192], BF16) as eb,
            nc.sbuf_tensor("vtl", [128, 2 * 256], BF16) as vtl,
            nc.sbuf_tensor("vst", [128, 256], BF16) as vstt,
            self.psums(6) as ps,
            nc.psum_tensor("psb", [128, 512], BF16) as psb,
            nc.psum_tensor("psb2", [128, 512], BF16) as psb2,
        ):
            self.T.update(w0T=w0T, slabT=slabT, xbf=xbf, xf32=xf32, stg=stg,
                          eb=eb, vtl=vtl, vstt=vstt, ps=ps, psb=psb,
                          psb2=psb2)
            self.phaseA_proj()
            self.phaseA_conv()
            P.barrier(S.ph)
            P.flush()
        with (
            nc.sbuf_tensor("wouT", [128, DC * 1024], BF16) as wouT,
            nc.sbuf_tensor("vmv", [128, DC * 4096], BF16) as vmv,
            nc.sbuf_tensor("pln", [128, DC * 512], F32) as pln,
            nc.sbuf_tensor("sqt", [128, 512], F32) as sqt,
            nc.sbuf_tensor("xrs", [128, 512], F32) as xrs,
            self.psums(8) as ps,
        ):
            self.T.update(wouT=wouT, vmv=vmv, pln=pln, sqt=sqt, xrs=xrs, ps=ps)
            self.phaseA_out()
            P.barrier(S.ph)
            P.flush()
        for l in range(2):
            sin = self.dd["sA"] if l == 0 else self.dd["sC"]
            sout = self.dd["sB"] if l == 0 else self.dd["out"]
            with (
                nc.sbuf_tensor(f"hx{l}", [128, HC * 514], F32) as hx,
                nc.sbuf_tensor(f"stl{l}", [128, 2 * DC * 512], F32) as stl,
                nc.sbuf_tensor(f"xq{l}", [128, DC * 512], BF16) as xq,
                nc.sbuf_tensor(f"hq{l}", [128, 1024], BF16) as hq,
                nc.sbuf_tensor(f"hs{l}", [128, 512], F32) as hs,
                nc.sbuf_tensor(f"wst{l}", [128, 2 * 2048], BF16) as wst,
                nc.sbuf_tensor(f"qsc{l}", [128, 3 * 512 + 16], F32) as qsc,
                nc.sbuf_tensor(f"mxa{l}", [128, 512], F32) as mxa,
                nc.sbuf_tensor(f"edg{l}", [128, HC * 2 + 2], F32) as edg,
                nc.sbuf_tensor(f"sq2{l}", [128, 512], F32) as sq2,
                self.psums(8) as ps,
            ):
                self.T.update(hx=hx, stl=stl, xq=xq, hq=hq, hs=hs, wst=wst,
                              qsc=qsc, mxa=mxa, edg=edg, sq2=sq2, ps=ps)
                self.phaseB(l, sin, sout)
                P.barrier(S.ph)
                P.flush()
            if l == 0:
                with (
                    nc.sbuf_tensor("xpw", [128, DC * 515], F32) as xpw,
                    nc.sbuf_tensor("xsw", [128, DC * 512], F32) as xsw,
                    nc.sbuf_tensor("xsb", [128, DC * 512], BF16) as xsb,
                    nc.sbuf_tensor("dtw", [128, DC * 512], F32) as dtw,
                    nc.sbuf_tensor("abf", [128, 2 * 2048], F32) as abf,
                    nc.sbuf_tensor("zbf", [128, 2 * 2048], F32) as zbf,
                    nc.sbuf_tensor("u1b", [128, 2048], F32) as u1b,
                    nc.sbuf_tensor("u2b", [128, 2048], F32) as u2b,
                    nc.sbuf_tensor("ubf", [128, 2 * 2048], F32) as ubf,
                    nc.sbuf_tensor("yw", [128, DC * 512], F32) as ywn,
                    nc.sbuf_tensor("wsc", [128, 2 * 2048], BF16) as wsc,
                    nc.sbuf_tensor("stlc", [128, DC * 512], F32) as stlc,
                    nc.sbuf_tensor("zst", [128, 1024], BF16) as zst,
                    nc.sbuf_tensor("msc", [128, 256], F32) as msc,
                    nc.sbuf_tensor("ysm", [128, 2048], F32) as ysm,
                    nc.sbuf_tensor("sow", [128, 4096], BF16) as sow,
                    nc.sbuf_tensor("zwin", [128, 4096], BF16) as zwin,
                    nc.psum_tensor("pstate", [128, 4096], F32) as pstate,
                ):
                    self.T.update(xpw=xpw, xsw=xsw, xsb=xsb, dtw=dtw, abf=abf,
                                  zbf=zbf, u1b=u1b, u2b=u2b, ubf=ubf,
                                  ywn=ywn, wsc=wsc, stlc=stlc, zst=zst, msc=msc,
                                  ysm=ysm, sow=sow, zwin=zwin, pstate=pstate)
                    self.phaseCD()
                    P.barrier(S.ph)
                    P.flush()
        P.wait("sync", S.st)
        P.barrier(S.fin)
        P.flush()

    PSCTR = [0]

    def psums(self, n):
        nc = self.nc
        ctr = self.PSCTR

        class _PS:
            def __init__(s):
                s.cms = []

            def __enter__(s):
                out = []
                ctr[0] += 1
                for i in range(n):
                    cm = nc.psum_tensor(f"ps{ctr[0]}_{i}", [128, 512], F32)
                    out.append(cm.__enter__())
                    s.cms.append(cm)
                return out

            def __exit__(s, *a):
                for cm in reversed(s.cms):
                    cm.__exit__(*a)
                return False
        return _PS()

    def prologue(self):
        dd, S, T, P, C = self.dd, self.S, self.T, self.P, self.C
        cst = T["cst"]

        def ld(dst, src):
            P.op("sync", lambda e, d=dst, s=src: e.dma_start(out=d, in_=s), S.ld, 16)

        ld(cst[:, C["A"]:C["A"] + 256], dd["A_sc"])
        ld(cst[:, C["wB"]:C["wB"] + 256], dd["wB_sc"])
        ld(cst[:, C["bB"]:C["bB"] + 256], dd["bB_sc"])
        ld(cst[:, C["wC"]:C["wC"] + 256], dd["wC_sc"])
        ld(cst[:, C["bC"]:C["bC"] + 256], dd["bC_sc"])
        ld(cst[:, C["cw4"]:C["cw4"] + 32], dd["cw4"])
        ld(cst[:, C["cb4"]:C["cb4"] + 8], dd["cb4"])
        ld(cst[:, C["b_dt"]:C["b_dt"] + 8], dd["b_dt"])
        ld(cst[:, C["b_out"]:C["b_out"] + 8], dd["b_out"])
        ld(cst[:, C["ln_g"]:C["ln_g"] + 8], dd["ln_g"])
        ld(cst[:, C["ln_b"]:C["ln_b"] + 8], dd["ln_b"])
        ld(cst[:, C["b_in"]:C["b_in"] + 16], dd["b_in"])
        ld(cst[:, C["fb2"]:C["fb2"] + 2], dd["f_b2s"])
        ld(cst[0:64, C["fb1"]:C["fb1"] + 2], dd["f_b1T"])
        ld(cst[0:1, C["fw1"]:C["fw1"] + 64], dd["f_w1"][0:1, :])
        ld(cst[0:1, C["fw1"] + 64:C["fw1"] + 128], dd["f_w1"][1:2, :])
        for l in range(2):
            ld(cst[:, C[f"ffn{l}_cw"]:C[f"ffn{l}_cw"] + 96], dd[f"ffn{l}_cw"])
            ld(cst[:, C[f"ffn{l}_cb"]:C[f"ffn{l}_cb"] + 32], dd[f"ffn{l}_cb"])

        P.wait("gpsimd", S.ld)
        P.op("gpsimd", lambda e: e.iota(T["smls"][:, 0:128].bitcast(I32),
                                        pattern=[[-1, 128]], base=0,
                                        channel_multiplier=1), S.gp, 1)
        P.wait("vector", S.gp)
        P.op("vector", lambda e: e.tensor_scalar(
            T["ident"][:], T["smls"][:, 0:128].bitcast(I32), 0.0, None,
            ALU.is_equal, ALU.bypass), S.ev, 1)
        P.op("vector", lambda e: e.tensor_copy(T["identb"][:], T["ident"][:]), S.ev, 1)
        P.op("vector", lambda e: e.memset(cst[:, C["ones"]:C["ones"] + 1], 1.0), S.ev, 1)
        P.op("vector", lambda e: e.memset(cst[:, C["oneN"]:C["oneN"] + 1], 1.0 / D), S.ev, 1)
        P.op("vector", lambda e: e.memset(cst[0:1, C["onesr"]:C["onesr"] + 128], 1.0), S.ev, 1)
        P.op("vector", lambda e: e.tensor_scalar(
            cst[:, C["A"]:C["A"] + 256], cst[:, C["A"]:C["A"] + 256], -5.0, 2.0,
            ALU.max, ALU.min), S.ev, 1)
        P.wait("scalar", S.ev)
        P.op("scalar", lambda e: e.activation(
            cst[:, C["A"]:C["A"] + 256], cst[:, C["A"]:C["A"] + 256], ACT.Exp,
            scale=1.0), S.evs, 1)
        P.wait("vector", S.evs)
        P.op("vector", lambda e: e.tensor_scalar(
            cst[:, C["A"]:C["A"] + 256], cst[:, C["A"]:C["A"] + 256], -1.0, None,
            ALU.mult, ALU.bypass), S.ev, 1)

        self.cast_stream("w0in_s", "w0T_n", 384, D)
        self.cast_stream("w0out", "w0out_b", D, D)
        self.cast_stream("w1in", "w1in_b", D, 2 * D)
        self.cast_stream("wdt", "wdt_b", D, D)
        self.cast_stream("w1outw", "w1out_b", D, D)
        for l in range(2):
            self.quant_ternary(f"ffn{l}_w1", f"w1q_n{l}", HID, D, C["sw"] + 2 * l)
            self.quant_ternary(f"ffn{l}_w2", f"w2q_n{l}", D, HID, C["sw"] + 2 * l + 1)
        P.barrier(S.ph)
        for l in range(2):
            self.transpose_dram(f"w1q_n{l}", f"w1qT{l}", HID, D)
            self.transpose_dram(f"w2q_n{l}", f"w2qT{l}", D, HID)
        self.hyena_filters()

    def cast_stream(self, src, dst, rows, cols):
        dd, S, T, P = self.dd, self.S, self.T, self.P
        buf = T["pA"]
        for r in range(cdiv(rows, 128)):
            rr = min(128, rows - r * 128)
            par = r % 2
            f32t = buf[:, par * cols:par * cols + cols]
            bft = buf[:, 2 * cols + par * (cols // 2):2 * cols + (par + 1) * (cols // 2)].bitcast(BF16)
            if r >= 2:
                P.wait("sync", S.st)
            P.op("sync", lambda e, t=f32t, r=r, rr=rr: e.dma_start(
                out=t[0:rr, :], in_=dd[src][r * 128:r * 128 + rr, :]), S.ld2, 16)
            P.wait("vector", S.ld2)
            P.op("vector", lambda e, a=bft, b=f32t, rr=rr: e.tensor_copy(
                a[0:rr, :], b[0:rr, :]), S.ev, 1)
            P.wait("sync", S.ev)
            P.op("sync", lambda e, a=bft, r=r, rr=rr: e.dma_start(
                out=dd[dst][r * 128:r * 128 + rr, :], in_=a[0:rr, :]), S.st, 16)
        P.wait("sync", S.st)

    def transpose_dram(self, src, dst, rows, cols):
        dd, S, T, P = self.dd, self.S, self.T, self.P
        buf = T["pA"]
        half = rows // 2
        for kc in range(cols // 128):
            par = kc % 2
            t = buf[:, par * half:(par + 1) * half].bitcast(BF16)
            if kc >= 2:
                P.wait("sync", S.st)
            P.op("sync", lambda e, t=t, kc=kc: e.dma_start_transpose(
                t, dd[src][:, kc * 128:(kc + 1) * 128]), S.tp, 16)
            P.wait("sync", S.tp)
            P.op("sync", lambda e, t=t, kc=kc: e.dma_start(
                out=dd[dst][kc * 128:(kc + 1) * 128, :], in_=t), S.st, 16)
        P.wait("sync", S.st)

    def quant_ternary(self, src, dst, rows, cols, sw_col):
        dd, S, T, P, C = self.dd, self.S, self.T, self.P, self.C
        bufB, bufA = T["pA"], T["pB"]
        ps = T["ps"]
        cst = T["cst"]
        sm = T["smls"]
        nrt = rows // 128
        acc = sm[:, 200:201]
        red = sm[:, 201:202]
        P.op("vector", lambda e: e.memset(acc, 0.0), S.ev, 1)
        for r in range(nrt):
            par = r % 2
            t = bufB[:, par * cols:(par + 1) * cols]
            if r >= 2:
                P.wait("sync", S.ev)
            P.op("sync", lambda e, t=t, r=r: e.dma_start(
                out=t, in_=dd[src][r * 128:(r + 1) * 128, :]), S.ld2, 16)
            P.wait("vector", S.ld2)
            P.op("vector", lambda e, t=t: e.tensor_reduce(
                red, t, AXX, ALU.add, apply_absolute_value=True), S.vx, 1)
            P.wait("vector", S.vx)
            P.op("vector", lambda e: e.tensor_tensor(acc, acc, red, ALU.add), S.vx, 1)
            P.wait("vector", S.vx)
            P.op("vector", lambda e: e.nop(), S.ev, 1)
        onec = cst[:, C["ones"]:C["ones"] + 1]
        P.wait("tensor", S.ev)
        P.op("tensor", lambda e: e.matmul(ps[0][0:1, 0:1], onec, acc,
                                          start=True, stop=True), S.mm, 1)
        P.wait("vector", S.mm)
        P.op("vector", lambda e: e.tensor_copy(sm[0:1, 202:203], ps[0][0:1, 0:1]), S.ev, 1)
        P.wait("vector", S.ev)
        P.wait("tensor", S.ev)
        P.op("tensor", lambda e: e.matmul(ps[1][:, 0:1], cst[0:1, C["onesr"]:C["onesr"] + 128],
                                          sm[0:1, 202:203], start=True, stop=True), S.mm, 1)
        inv = sm[:, 203:204]
        P.wait("vector", S.mm)
        self.vop(lambda e: e.tensor_scalar(
            cst[:, sw_col:sw_col + 1], ps[1][:, 0:1], 1.0 / (rows * cols), 1e-5,
            ALU.mult, ALU.max))
        P.op("vector", lambda e: e.reciprocal(inv, cst[:, sw_col:sw_col + 1]), S.ev, 1)
        P.wait("vector", S.ev)
        for r in range(nrt):
            par = r % 2
            t = bufB[:, par * cols:(par + 1) * cols]
            z2 = bufA[:, 0:cols]
            ti = bufA[:, cols:2 * cols].bitcast(I32)
            g = bufA[:, 2 * cols:3 * cols]
            ob = bufA[:, 3 * cols:3 * cols + cols // 2].bitcast(BF16)
            if r >= 2:
                P.wait("sync", S.ev)
            P.op("sync", lambda e, t=t, r=r: e.dma_start(
                out=t, in_=dd[src][r * 128:(r + 1) * 128, :]), S.ld2, 16)
            P.wait("vector", S.ld2)
            P.op("vector", lambda e, t=t, z2=z2: e.tensor_scalar(
                z2, t, inv, 0.5, ALU.mult, ALU.add), None)
            P.op("vector", lambda e, z2=z2, ti=ti: e.tensor_copy(ti, z2), None)
            P.op("vector", lambda e, z2=z2, ti=ti, g=g: e.tensor_tensor(
                g, ti, z2, ALU.is_gt), None)
            P.op("vector", lambda e, ti=ti, g=g: e.tensor_tensor(
                g, ti, g, ALU.subtract), None)
            P.op("vector", lambda e, g=g, ob=ob: e.tensor_scalar(
                ob, g, -1.0, 1.0, ALU.max, ALU.min), S.ev, 1)
            P.wait("sync", S.ev)
            P.op("sync", lambda e, ob=ob, r=r: e.dma_start(
                out=dd[dst][r * 128:(r + 1) * 128, :], in_=ob), S.st, 16)
        P.wait("sync", S.st)

    def hyena_filters(self):
        dd, S, T, P, C = self.dd, self.S, self.T, self.P, self.C
        cst = T["cst"]
        ps = T["ps"]
        sm = T["smls"]
        pU = T["pU"]
        zt = T["pA"][:, 0:1088].bitcast(BF16)
        P.wait("vector", S.st)
        P.wait("vector", S.tp)
        P.wait("vector", S.ld2)
        P.op("vector", lambda e: e.memset(zt, 0.0), S.ev, 1)
        P.wait("sync", S.ev)
        for st in range(2):
            for o in range(0, HPL, 2176):
                P.op("sync", lambda e, st=st, o=o: e.dma_start(
                    out=dd["hpad"][st, :, o:o + 2176], in_=zt), S.st, 16)
        P.op("sync", lambda e: e.dma_start(out=sm[:, 256:384], in_=dd["f_w2s"]), S.ld2, 16)
        P.wait("tensor", S.ld2)
        P.wait("tensor", S.ev)
        for st in range(2):
            P.op("tensor", lambda e, st=st: e.transpose(
                ps[6][0:64, st * 128:(st + 1) * 128],
                sm[:, 256 + st * 64:256 + (st + 1) * 64],
                T["ident"][:]), S.mm, 1)
        P.wait("vector", S.mm)
        fw2Tb = sm[:, 384:512].bitcast(BF16)
        P.op("vector", lambda e: e.tensor_copy(fw2Tb[0:64, :], ps[6][0:64, 0:256]),
             S.ev, 1)
        P.op("gpsimd", lambda e: e.iota(sm[0:1, 448:448 + L // 32].bitcast(I32),
                                        pattern=[[1, 128]], base=0,
                                        channel_multiplier=0), S.gp, 1)
        P.op("gpsimd", lambda e: e.iota(pU[0:1, 0:L].bitcast(I32),
                                        pattern=[[1, L]], base=0,
                                        channel_multiplier=0), S.gp, 1)
        trow = pU[0:1, L:2 * L]
        U = T["pB"][:, 0:2048].bitcast(BF16)
        hrow = T["pB"][:, 2048:2048 + 512]
        hrb = T["pB"][:, 2560:2560 + 256].bitcast(BF16)
        for st in range(2):
            P.wait("scalar", S.gp)
            P.wait("scalar", S.ev)
            if st == 1:
                P.wait("scalar", S.st, S.st.n)
            sc = 1.0 / (L - 1) if st == 0 else -1.0 / (L - 1)
            bi = 0.0 if st == 0 else 1.0
            P.op("scalar", lambda e, sc=sc, bi=bi: e.activation(
                trow, pU[0:1, 0:L].bitcast(I32), ACT.Copy, scale=sc, bias=bi),
                S.evs, 1)
            for nn in range(L // 512):
                bank = nn % 2
                P.wait("tensor", S.evs)
                P.op("tensor", lambda e, st=st, nn=nn, bank=bank: e.matmul(
                    ps[bank][0:64, 0:512],
                    cst[0:1, C["fw1"] + 64 * st:C["fw1"] + 64 * st + 64],
                    trow[:, nn * 512:(nn + 1) * 512],
                    start=True, stop=True), S.mm, 1)
                P.wait("scalar", S.mm)
                P.op("scalar", lambda e, st=st, nn=nn, bank=bank: e.activation(
                    U[0:64, nn * 512:(nn + 1) * 512], ps[bank][0:64, 0:512],
                    ACT.Silu, bias=cst[0:64, C["fb1"] + st:C["fb1"] + st + 1],
                    scale=1.0), S.evs, 1)
            for nn in range(L // 512):
                bank = 2 + nn % 2
                P.wait("tensor", S.evs)
                if nn >= 2:
                    P.wait("tensor", S.st)
                P.op("tensor", lambda e, st=st, nn=nn, bank=bank: e.matmul(
                    ps[bank][:, 0:512],
                    fw2Tb[0:64, st * 128:(st + 1) * 128],
                    U[0:64, nn * 512:(nn + 1) * 512],
                    start=True, stop=True), S.mm, 1)
                P.wait("vector", S.mm)
                P.op("vector", lambda e, st=st, nn=nn, bank=bank: e.tensor_scalar(
                    hrb, ps[bank][:, 0:512],
                    cst[:, C["fb2"] + st:C["fb2"] + st + 1], None,
                    ALU.add, ALU.bypass), S.ev, 1)
                off = (127 if st == 0 else 0) + nn * 512
                P.wait("sync", S.ev)
                P.op("sync", lambda e, st=st, off=off: e.dma_start(
                    out=dd["hpad"][st, :, off:off + 512], in_=hrb), S.st, 16)
                P.wait("vector", S.st)
        P.wait("sync", S.st)

    def phaseA_proj(self):
        dd, S, T, P = self.dd, self.S, self.T, self.P
        w0T, xbf, xf32, stg, ps = T["w0T"], T["xbf"], T["xf32"], T["stg"], T["ps"]
        for kc in range(DC):
            P.op("sync", lambda e, kc=kc: e.dma_start_transpose(
                w0T[:, kc * 384:(kc + 1) * 384],
                dd["w0T_n"][:, kc * 128:(kc + 1) * 128]), S.tp, 16)
        P.wait("sync", S.tp)
        NT = B * L // 512
        for n in range(NT):
            par = n % 2
            xb = xbf[:, par * DC * 512:(par + 1) * DC * 512]
            for kc in range(DC):
                xs = xf32[:, (kc % 2) * 512:((kc % 2) + 1) * 512]
                if n + kc > 0:
                    P.wait("sync", S.ev)
                P.op("sync", lambda e, xs=xs, kc=kc, n=n: e.dma_start(
                    out=xs, in_=dd["xfull"][kc * 128:(kc + 1) * 128,
                                            n * 512:(n + 1) * 512]), S.ld, 16)
                P.wait("vector", S.ld)
                P.op("vector", lambda e, xs=xs, xb=xb, kc=kc: e.tensor_copy(
                    xb[:, kc * 512:(kc + 1) * 512], xs), S.ev, 1)
            P.wait("tensor", S.ev)
            for m in range(3):
                for kc in range(DC):
                    P.op("tensor", lambda e, m=m, kc=kc, xb=xb: e.matmul(
                        ps[m][:, 0:512],
                        w0T[:, kc * 384 + m * 128:kc * 384 + (m + 1) * 128],
                        xb[:, kc * 512:(kc + 1) * 512],
                        start=(kc == 0), stop=(kc == DC - 1)),
                        S.mm if kc == DC - 1 else None, 1)
            P.wait("vector", S.mm)
            if n >= 1:
                P.wait("vector", S.stv)
            for m in range(3):
                dst = stg[:, m * 512:(m + 1) * 512]
                if m == 1:
                    P.op("vector", lambda e, m=m, dst=dst: e.tensor_copy(
                        dst, ps[m][:, 0:512]), S.ev, 1)
                else:
                    rdst = rap(stg[:], m * 512 + 127, [[1536, 128], [128, 4], [-1, 128]])
                    P.op("vector", lambda e, m=m, rdst=rdst: e.tensor_copy(
                        rdst, ps[m][:, 0:512]), S.ev, 1)
            P.wait("sync", S.ev)
            for m in range(3):
                P.op("sync", lambda e, m=m, n=n: e.dma_start(
                    out=dd["projslab"][m * 128:(m + 1) * 128, n * 512:(n + 1) * 512],
                    in_=stg[:, m * 512:(m + 1) * 512]), S.stv, 16)
        P.wait("sync", S.stv)

    def phaseA_conv(self):
        dd, S, T, P = self.dd, self.S, self.T, self.P
        slabT, eb, vtl, ps, psb = T["slabT"], T["eb"], T["vtl"], T["ps"], T["psb"]
        psb2 = T["psb2"]
        identb = T["identb"]
        slab = slabT[:]
        ldeng = ("sync", "gpsimd")
        ldsplit = ((0, 64), (64, 128))
        ldc = {}
        for s in range(2):
            P.wait("sync", S.ev)
            P.wait("sync", S.mm)
            if s > 0:
                for eng in ldeng[1:]:
                    P.wait(eng, S.mm)
                    if eng != "vector":
                        P.wait(eng, S.ev, S.ev.n)
            else:
                for eng in ldeng[1:]:
                    P.wait(eng, S.stv, S.stv.n)
            for m in range(3):
                for tb in range(256):
                    P.op("sync", lambda e, m=m, tb=tb, s=s: e.dma_start_transpose(
                        slabT[:, tb * 192 + m * 64:tb * 192 + m * 64 + 64],
                        dd["projslab"][m * 128 + s * 64:m * 128 + s * 64 + 64,
                                       tb * 128:(tb + 1) * 128]),
                        S.tp, 16)
            P.wait("sync", S.tp)
            P.wait("tensor", S.tp)

            ldsems = [S.ldw, S.ld, S.ld2]

            def eload(d, s, buf):
                for qi, eng in enumerate(ldeng):
                    p0, p1 = ldsplit[qi]
                    P.op(eng, lambda e, d=d, p0=p0, p1=p1, buf=buf: e.dma_start(
                        out=buf[p0:p1, 0:4096],
                        in_=rap(dd["hpad"], d * HPL + p0,
                                [[1, p1 - p0], [1, 4096]])), ldsems[qi], 16)
                    P.op(eng, lambda e, d=d, p0=p0, p1=p1, buf=buf: e.dma_start(
                        out=buf[p0:p1, 4096:8192],
                        in_=rap(dd["hpad"], (128 + d) * HPL + p0,
                                [[1, p1 - p0], [1, 4096]])), ldsems[qi], 16)
                ldc[(s, d)] = tuple(sem.n for sem in ldsems)

            mmc = {}
            eload(s * 64, s, eb[:, 0:8192])
            for d2 in range(64):
                d = s * 64 + d2
                par = d2 % 2
                e1 = eb[:, par * 8192:par * 8192 + 4096]
                e2 = eb[:, par * 8192 + 4096:par * 8192 + 8192]
                if d2 + 1 < 64:
                    if d2 >= 1:
                        for eng in ldeng:
                            P.wait(eng, S.mm, mmc[d2 - 1])
                    eload(d + 1, s, eb[:, (1 - par) * 8192:(2 - par) * 8192])
                for qi, sem in enumerate(ldsems):
                    P.wait("tensor", sem, ldc[(s, d)][qi])
                P.wait("tensor", S.ev)
                for dl in range(32):
                    mov = rap(slab, d2, [[49152, 128], [32 * 192, 8], [192, 32 - dl]])
                    outap = rap(ps[0][:], dl, [[512, 128], [32, 8], [1, 32 - dl]])
                    P.op("tensor", lambda e, mov=mov, outap=outap, e1=e1, dl=dl: e.matmul(
                        outap, e1[:, dl * 128:(dl + 1) * 128], mov,
                        start=(dl == 0), stop=(dl == 31)),
                        S.mm if dl == 31 else None, 1)
                v1 = vtl[:, 0:256]
                x1ap = rap(slab, 64 + d2, [[49152, 128], [32 * 192, 8], [192, 32]])
                P.wait("vector", S.mm)
                P.op("vector", lambda e, v1=v1, x1ap=x1ap: e.tensor_tensor(
                    v1, ps[0][:, 0:256], x1ap, ALU.mult), S.ev, 1)
                P.wait("tensor", S.ev)
                for dl in range(32):
                    mov = rap(vtl[:], 0, [[512, 128], [32, 8], [1, 32 - dl]])
                    outap = rap(ps[1][:], dl, [[512, 128], [32, 8], [1, 32 - dl]])
                    base = 3968 - 128 * dl
                    P.op("tensor", lambda e, mov=mov, outap=outap, e2=e2, base=base, dl=dl: e.matmul(
                        outap, e2[:, base:base + 128], mov,
                        start=(dl == 0), stop=(dl == 31)),
                        S.mm if dl == 31 else None, 1)
                mmc[d2] = S.mm.n
                v2 = vtl[:, 256:512]
                x2ap = rap(slab, 128 + d2, [[49152, 128], [32 * 192, 8], [192, 32]])
                P.wait("vector", S.mm)
                P.op("vector", lambda e, v2=v2, x2ap=x2ap: e.tensor_tensor(
                    v2, ps[1][:, 0:256], x2ap, ALU.mult), S.ev, 1)
                vst = T["vstt"][:]
                P.wait("tensor", S.ev)
                P.op("tensor", lambda e, v2=v2: e.transpose(
                    psb[:, 0:128], v2[:, 0:128], identb[:]), None)
                P.op("tensor", lambda e, v2=v2: e.transpose(
                    psb2[:, 0:128], v2[:, 128:256], identb[:]), S.mm, 1)
                P.wait("vector", S.mm)
                P.wait("vector", S.stv)
                for h, psrc in ((0, psb), (1, psb2)):
                    rvst = rap(vst, h * 128 + 127, [[256, 128], [-1, 128]])
                    P.op("vector", lambda e, rvst=rvst, psrc=psrc: e.tensor_copy(
                        rvst, psrc[:, 0:128]), S.ev, 1)
                P.wait("sync", S.ev)
                for h in range(2):
                    P.op("sync", lambda e, vst=vst, d=d, h=h: e.dma_start(
                        out=dd["vg_in"][d, h * 128:(h + 1) * 128, :],
                        in_=vst[:, h * 128:(h + 1) * 128]), S.stv, 16)
        P.wait("sync", S.stv)
        if "vg_in" in dd.get("_dumps", ()):
            P.op("sync", lambda e: e.dma_start(out=dd["vg_in_dbg"], in_=dd["vg_in"]), S.st, 16)
            P.wait("sync", S.st)
        P.barrier(S.ph)
        P.op("gpsimd", lambda e: e.collective_compute(
            "AllGather", ALU.bypass, ins=[dd["vg_in"]], outs=[dd["vg_out"]],
            replica_groups=[list(range(8))]), S.cc, 1)
        for eng in ("sync", "vector", "scalar", "tensor"):
            P.wait(eng, S.cc)

    def phaseA_out(self):
        dd, S, T, P, C = self.dd, self.S, self.T, self.P, self.C
        wouT, vmv, pln, ps = T["wouT"], T["vmv"], T["pln"], T["ps"]
        cell = [None, None]
        dsd = {}

        def grab_cid(e):
            cell[0] = e.partition_id()
            cell[1] = cell[0] * 32
            return e.nop()
        P.op("sync", grab_cid, None)
        for kc in range(DC):
            P.op("sync", lambda e, kc=kc: e.dma_start_transpose(
                wouT[:, kc * 1024:(kc + 1) * 1024],
                dd["w0out_b"][:, kc * 128:(kc + 1) * 128]), S.tp, 16)
        P.wait("sync", S.tp)
        for r in range(DC):
            def ldv(e, r=r):
                if 0 not in dsd:
                    dsd[0] = bass.ds(cell[1], 32)
                sl = dd["vg_out"][r][:, dsd[0], :]
                out = vmv[:, r * 4096:(r + 1) * 4096].rearrange(
                    "p (j n) -> p j n", j=32)
                return e.dma_start(out=out, in_=sl)
            P.op("sync", ldv, S.ld, 16)
        if "a3in" in dd.get("_dumps", ()):
            P.wait("sync", S.ld)
            P.op("sync", lambda e: e.dma_start(out=dd["a3in"], in_=vmv[:]), S.st, 16)
        for tt in range(NTT):
            P.wait("tensor", S.ld)
            P.wait("tensor", S.ev)
            for m in range(DC):
                for kc in range(DC):
                    P.op("tensor", lambda e, m=m, kc=kc, tt=tt: e.matmul(
                        ps[m][:, 0:512],
                        wouT[:, kc * 1024 + m * 128:kc * 1024 + (m + 1) * 128],
                        vmv[:, kc * 4096 + tt * 512:kc * 4096 + (tt + 1) * 512],
                        start=(kc == 0), stop=(kc == DC - 1)),
                        S.mm if kc == DC - 1 else None, 1)
            P.wait("vector", S.mm)
            for m in range(DC):
                xrs = T["xrs"][:]
                P.wait("sync", S.ev)
                P.op("sync", lambda e, m=m, tt=tt, xrs=xrs: e.dma_start(
                    out=xrs, in_=dd["xmine"][m * 128:(m + 1) * 128,
                                             tt * 512:(tt + 1) * 512]), S.ldx, 16)
                P.wait("vector", S.ldx)
                P.op("vector", lambda e, m=m, xrs=xrs: e.tensor_tensor(
                    pln[:, m * 512:(m + 1) * 512], ps[m][:, 0:512], xrs, ALU.add),
                    S.ev, 1)
            if "preA" in dd.get("_dumps", ()):
                P.wait("sync", S.ev)
                for m in range(DC):
                    P.op("sync", lambda e, m=m, tt=tt: e.dma_start(
                        out=dd["preA"][m * 128:(m + 1) * 128, tt * 512:(tt + 1) * 512],
                        in_=pln[:, m * 512:(m + 1) * 512]), S.st, 16)
            self.layer_norm(pln, T["sqt"][:], 512, affine=False,
                            dram=dd["sA"], dram_col0=tt * 512)
        P.wait("sync", S.st)

    def layer_norm(self, pln, sqt, width, affine, dst_sb=None,
                   dram=None, dram_col0=None, store_lo=0, store_hi=None,
                   pstats=None, cstride=None, coff=0):
        dd, S, T, P, C = self.dd, self.S, self.T, self.P, self.C
        cst, sm = T["cst"], T["smls"]
        if pstats is None:
            ps = T["ps"]
            ps6, ps7 = ps[6][:, 0:512], ps[7][:, 0:512]
            ppitch = 512
            ps6_off = 0
        else:
            ps6, ps7, ppitch, ps6_off = pstats
        if cstride is None:
            cstride = width

        def pch(m):
            return pln[:, m * cstride + coff:m * cstride + coff + width]
        ntc = width // 128
        if store_hi is None:
            store_hi = width
        for m in range(DC):
            P.wait("scalar", S.ev)
            if m > 0:
                P.wait("scalar", S.mm)
            P.op("scalar", lambda e, m=m: e.activation(
                sqt[:, 0:width], pch(m), ACT.Square,
                scale=1.0), S.evs, 1)
            P.wait("tensor", S.evs)
            if m == 0:
                P.wait("tensor", S.aux)
            for tc in range(ntc):
                P.op("tensor", lambda e, m=m, tc=tc: e.matmul(
                    ps6[:, 2 * tc:2 * tc + 1],
                    pch(m)[:, tc * 128:(tc + 1) * 128],
                    cst[:, C["oneN"]:C["oneN"] + 1],
                    start=(m == 0 and tc == 0), stop=(m == DC - 1)), None)
                P.op("tensor", lambda e, m=m, tc=tc: e.matmul(
                    ps6[:, 2 * tc + 1:2 * tc + 2],
                    sqt[:, tc * 128:(tc + 1) * 128],
                    cst[:, C["oneN"]:C["oneN"] + 1],
                    start=False, stop=(m == DC - 1)),
                    S.mm if tc == ntc - 1 else None, 1)
        mu = sm[:, 512:512 + ntc]
        msq = sm[:, 516:516 + ntc]
        var = sm[:, 520:520 + ntc]
        r0 = sm[:, 524:524 + ntc]
        t1 = sm[:, 528:528 + ntc]
        P.wait("vector", S.mm)
        self.vop(lambda e: e.tensor_copy(
            mu, rap(ps6, ps6_off, [[ppitch, 128], [2, ntc]])))
        self.vop(lambda e: e.tensor_copy(
            msq, rap(ps6, ps6_off + 1, [[ppitch, 128], [2, ntc]])))
        self.vop(lambda e: e.tensor_tensor(var, mu, mu, ALU.mult))
        self.vop(lambda e: e.tensor_tensor(var, msq, var, ALU.subtract))
        P.op("vector", lambda e: e.tensor_scalar(var, var, LNEPS, None, ALU.add,
                                                 ALU.bypass), S.ev, 1)
        P.wait("scalar", S.ev)
        P.op("scalar", lambda e: e.activation(r0, var, ACT.Sqrt, scale=1.0), S.evs, 1)
        P.wait("vector", S.evs)
        self.vop(lambda e: e.reciprocal(r0, r0))
        self.vop(lambda e: e.tensor_tensor(t1, r0, r0, ALU.mult))
        self.vop(lambda e: e.tensor_tensor(t1, t1, var, ALU.mult))
        self.vop(lambda e: e.tensor_scalar(t1, t1, -0.5, 1.5, ALU.mult, ALU.add))
        self.vop(lambda e: e.tensor_tensor(r0, r0, t1, ALU.mult))
        self.vop(lambda e: e.tensor_copy(sm[:, 640:640 + ntc], mu))
        P.op("vector", lambda e: e.tensor_copy(sm[:, 640 + ntc:640 + 2 * ntc], r0),
             S.ev, 1)
        if "lnr2" in dd.get("_dumps", ()) and not dd.get("_lnr2_done"):
            dd["_lnr2_done"] = True
            P.wait("sync", S.ev)
            P.op("sync", lambda e: e.dma_start(out=dd["lnr2"], in_=sm[:, 512:544]), S.st, 16)
            P.wait("sync", S.st)
            P.wait("vector", S.st)
        P.wait("tensor", S.ev)
        P.op("tensor", lambda e: e.transpose(ps7[:, 0:128],
                                             sm[:, 640:768], T["ident"][:]), S.mm, 1)
        P.wait("vector", S.mm)
        P.op("vector", lambda e: e.tensor_copy(
            sm[0:2 * ntc, 2176:2304], ps7[0:2 * ntc, 0:128]), S.ev, 1)
        P.wait("sync", S.ev)
        P.op("sync", lambda e: e.dma_start(
            out=sm[0:1, 768:768 + 2 * ntc * 128],
            in_=sm[0:2 * ntc, 2176:2304]), S.ld2, 16)
        murow = sm[0:1, 768:768 + width]
        rsrow = sm[0:1, 768 + width:768 + 2 * width]
        if "lnrow" in dd.get("_dumps", ()) and not dd.get("_lnrow_done"):
            dd["_lnrow_done"] = True
            P.wait("sync", S.ld2)
            P.op("sync", lambda e: e.dma_start(out=dd["lnrow"].rearrange("p c -> (p c)").unsqueeze(0), in_=sm[0:1, 768:1792]), S.st, 16)
            P.wait("sync", S.st)
        P.wait("tensor", S.ld2)
        P.op("tensor", lambda e: e.matmul(ps6[:, 0:width],
                                          cst[0:1, C["onesr"]:C["onesr"] + 128],
                                          murow, start=True, stop=True), None)
        P.op("tensor", lambda e: e.matmul(ps7[:, 0:width],
                                          cst[0:1, C["onesr"]:C["onesr"] + 128],
                                          rsrow, start=True, stop=True), S.mm, 1)
        P.wait("vector", S.mm)
        scr = sqt[:, 0:width]
        for m in range(DC):
            src = pch(m)
            P.op("vector", lambda e, src=src, scr=scr: e.tensor_tensor(
                scr, src, ps6[:, 0:width], ALU.subtract), None)
            if dst_sb is not None:
                dstm = dst_sb[:, m * cstride + coff:m * cstride + coff + width]
                P.op("vector", lambda e, scr=scr, dstm=dstm: e.tensor_tensor(
                    dstm, scr, ps7[:, 0:width], ALU.mult),
                    None if affine else S.ev, 1)
                if affine:
                    P.op("vector", lambda e, dstm=dstm, m=m: e.tensor_scalar(
                        dstm, dstm, cst[:, C["ln_g"] + m:C["ln_g"] + m + 1],
                        cst[:, C["ln_b"] + m:C["ln_b"] + m + 1],
                        ALU.mult, ALU.add), S.ev, 1)
            else:
                out = sqt[:, 0:width]
                P.op("vector", lambda e, scr=scr: e.tensor_tensor(
                    scr, scr, ps7[:, 0:width], ALU.mult),
                    None if affine else S.ev, 1)
                if affine:
                    P.op("vector", lambda e, scr=scr, m=m: e.tensor_scalar(
                        scr, scr, cst[:, C["ln_g"] + m:C["ln_g"] + m + 1],
                        cst[:, C["ln_b"] + m:C["ln_b"] + m + 1],
                        ALU.mult, ALU.add), S.ev, 1)
                P.wait("sync", S.ev)
                P.op("sync", lambda e, scr=scr, m=m: e.dma_start(
                    out=dram[m * 128:(m + 1) * 128,
                             dram_col0 + store_lo:dram_col0 + store_hi],
                    in_=scr[:, store_lo:store_hi]), S.st, 16)
                P.wait("vector", S.st)
        P.wait("tensor", S.ev)
        P.op("tensor", lambda e: e.nop(), S.aux, 1)

    def phaseB(self, l, sin, sout):
        dd, S, T, P, C = self.dd, self.S, self.T, self.P, self.C
        cst = T["cst"]
        hx, stl, xq, hq, hs = T["hx"], T["stl"], T["xq"], T["hq"], T["hs"]
        wst, qsc, mxa, edg, sq2 = T["wst"], T["qsc"], T["mxa"], T["edg"], T["sq2"]
        ps = T["ps"]
        swx = C["sw"] + 2 * l
        swh = C["sw"] + 2 * l + 1
        cwc = C[f"ffn{l}_cw"]
        cbc = C[f"ffn{l}_cb"]
        lnv = {}
        for tt in range(NTT + 1):
            flush = (tt == NTT)
            stl_t = stl[:, (tt % 2) * (DC * 512):(tt % 2 + 1) * (DC * 512)]
            wtok = 128 if flush else 512
            a0 = tt * 512 - 1
            if tt >= 2:
                P.wait("sync", S.ev, lnv[tt - 2])
            else:
                P.wait("sync", S.ev)
            if flush:
                P.op("vector", lambda e, stl_t=stl_t: e.memset(stl_t[:, 0:DC * 512], 0.0), S.ev, 1)
                P.wait("sync", S.ev)
                for m in range(DC):
                    P.op("sync", lambda e, m=m, stl_t=stl_t: e.dma_start(
                        out=stl_t[:, m * 512:m * 512 + 1],
                        in_=sin[m * 128:(m + 1) * 128, 4095:4096],
                        allow_slow_non_contiguous=True), S.ld, 16)
            else:
                for m in range(DC):
                    if tt == 0:
                        P.op("vector", lambda e, m=m, stl_t=stl_t: e.memset(
                            stl_t[:, m * 512:m * 512 + 1], 0.0), S.ev, 1)
                        P.wait("sync", S.ev, S.ev.n)
                        P.op("sync", lambda e, m=m, stl_t=stl_t: e.dma_start(
                            out=stl_t[:, m * 512 + 1:m * 512 + 512],
                            in_=sin[m * 128:(m + 1) * 128, 0:511]), S.ld, 16)
                    else:
                        P.op("sync", lambda e, m=m, a0=a0, stl_t=stl_t: e.dma_start(
                            out=stl_t[:, m * 512:(m + 1) * 512],
                            in_=sin[m * 128:(m + 1) * 128, a0:a0 + 512]), S.ld, 16)
            P.wait("vector", S.ld)
            self.quantize(stl_t, 512, DC, xq, qsc, mxa, sq2, hs,
                          sw_col=swx, out_slot_w=512)
            if tt == 0 or flush:
                if tt == 0:
                    P.op("vector", lambda e: e.memset(
                        rap(xq[:], 0, [[4096, 128], [512, DC], [1, 1]]), 0.0), S.ev, 1)
                else:
                    P.op("vector", lambda e: e.memset(
                        rap(xq[:], 1, [[4096, 128], [512, DC], [1, 511]]), 0.0), S.ev, 1)
            P.wait("vector", S.mm)
            if tt == 0:
                P.op("vector", lambda e: e.memset(edg[:, 0:2 * HC], 0.0), S.ev, 1)
            edsrc = rap(hx[:], 512, [[HC * 514, 128], [514, HC], [1, 2]])
            eddst = rap(edg[:], 0, [[2 * HC + 2, 128], [2, HC], [1, 2]])
            P.op("vector", lambda e, eddst=eddst: e.tensor_copy(
                rap(hx[:], 0, [[HC * 514, 128], [514, HC], [1, 2]]), eddst), S.ev, 1)
            P.op("vector", lambda e: e.memset(mxa[:, 0:512], 0.0), S.ev, 1)
            evw1 = {}
            w1mm = {}
            for hc in range(HC):
                par = hc % 2
                wsl = wst[:, par * 1024:(par + 1) * 1024]
                if hc >= 2:
                    P.wait("sync", S.mm, w1mm[hc - 2])
                P.op("sync", lambda e, hc=hc, wsl=wsl: e.dma_start(
                    out=wsl, in_=rap(dd[f"w1qT{l}"], hc * 128,
                                     [[HID, 128], [128 * HID, DC], [1, 128]])),
                    S.ldw, 16)
                P.wait("tensor", S.ldw)
                if hc >= 2:
                    P.wait("tensor", S.ev, evw1[hc - 2])
                else:
                    P.wait("tensor", S.ev)
                for kc in range(DC):
                    P.op("tensor", lambda e, kc=kc, wsl=wsl, par=par: e.matmul(
                        ps[par][:, 0:512], wsl[:, kc * 128:(kc + 1) * 128],
                        xq[:, kc * 512:(kc + 1) * 512],
                        start=(kc == 0), stop=(kc == DC - 1)),
                        S.mm if kc == DC - 1 else None, 1)
                w1mm[hc] = S.mm.n
                P.wait("vector", S.mm)
                hslice = hx[:, hc * 514 + 2:hc * 514 + 514]
                P.op("vector", lambda e, hslice=hslice, par=par: e.scalar_tensor_tensor(
                    hslice, ps[par][:, 0:512], qsc[:, 1536:1537],
                    qsc[:, 512:1024], ALU.mult, ALU.mult), S.ev, 1)
                evw1[hc] = S.ev.n
            P.op("vector", lambda e, edsrc=edsrc, eddst=eddst: e.tensor_copy(
                eddst, edsrc), S.ev, 1)
            for hc in range(HC):
                hxc = hx[:, hc * 514:hc * 514 + 514]
                P.wait("vector", S.ev, S.ev.n)
                P.op("vector", lambda e, hxc=hxc, hc=hc: e.tensor_scalar(
                    sq2[:, 0:512], hxc[:, 0:512],
                    cst[:, cwc + hc * 3:cwc + hc * 3 + 1], None,
                    ALU.mult, ALU.bypass), None)
                P.op("vector", lambda e, hxc=hxc, hc=hc: e.scalar_tensor_tensor(
                    hs[:, 0:512], hxc[:, 1:513],
                    cst[:, cwc + hc * 3 + 1:cwc + hc * 3 + 2],
                    sq2[:, 0:512], ALU.mult, ALU.add), None)
                P.op("vector", lambda e, hxc=hxc, hc=hc: e.scalar_tensor_tensor(
                    sq2[:, 0:512], hxc[:, 2:514],
                    cst[:, cwc + hc * 3 + 2:cwc + hc * 3 + 3],
                    hs[:, 0:512], ALU.mult, ALU.add), S.ev, 1)
                P.wait("scalar", S.ev)
                P.op("scalar", lambda e, hxc=hxc, hc=hc: e.activation(
                    hxc[:, 2:514], sq2[:, 0:512], ACT.Silu,
                    bias=cst[:, cbc + hc:cbc + hc + 1], scale=1.0), S.evs, 1)
                P.wait("vector", S.evs)
                P.op("vector", lambda e, hxc=hxc: e.tensor_tensor(
                    mxa[:, 0:512], mxa[:, 0:512], hxc[:, 2:514], ALU.max), None)
                P.op("vector", lambda e, hxc=hxc: e.scalar_tensor_tensor(
                    mxa[:, 0:512], hxc[:, 2:514], -1.0, mxa[:, 0:512],
                    ALU.mult, ALU.max), S.ev, 1)
            self.qscales(mxa, qsc, swh)
            w1mm_last = S.mm.n
            w2mm = {}
            qev = {}
            for hc in range(HC):
                par = hc % 2
                wsl = wst[:, par * 1024:(par + 1) * 1024]
                hqs = hq[:, par * 512:(par + 1) * 512]
                hxc = hx[:, hc * 514 + 2:hc * 514 + 514]
                if hc >= 2:
                    P.wait("vector", S.mm, w2mm[hc - 2])
                P.op("vector", lambda e, hxc=hxc: e.tensor_tensor(
                    sq2[:, 0:512], hxc, qsc[:, 0:512], ALU.mult), None)
                P.op("vector", lambda e: e.tensor_scalar(
                    hs[:, 0:512], sq2[:, 0:512], 0.5, None, ALU.add, ALU.bypass), None)
                P.op("vector", lambda e: e.tensor_copy(
                    mxa[:, 0:512].bitcast(I32)[:, 0:512], hs[:, 0:512]), None)
                P.op("vector", lambda e: e.tensor_tensor(
                    sq2[:, 0:512], mxa[:, 0:512].bitcast(I32)[:, 0:512],
                    hs[:, 0:512], ALU.is_gt), None)
                P.op("vector", lambda e: e.tensor_tensor(
                    hs[:, 0:512], mxa[:, 0:512].bitcast(I32)[:, 0:512],
                    sq2[:, 0:512], ALU.subtract), None)
                P.op("vector", lambda e, hqs=hqs: e.tensor_scalar(
                    hqs, hs[:, 0:512], -128.0, 127.0, ALU.max, ALU.min),
                    S.ev, 1)
                qev[hc] = S.ev.n
                P.wait("sync", S.mm, w2mm[hc - 2] if hc >= 2 else w1mm_last)
                P.op("sync", lambda e, hc=hc, wsl=wsl: e.dma_start(
                    out=wsl, in_=dd[f"w2qT{l}"][hc * 128:(hc + 1) * 128, :]),
                    S.ldw, 16)
                P.wait("tensor", S.ldw)
                P.wait("tensor", S.ev, qev[hc])
                for m in range(DC):
                    P.op("tensor", lambda e, m=m, wsl=wsl, hc=hc, hqs=hqs: e.matmul(
                        ps[m][:, 0:512], wsl[:, m * 128:(m + 1) * 128],
                        hqs,
                        start=(hc == 0), stop=(hc == HC - 1)),
                        S.mm if m == DC - 1 else None, 1)
                w2mm[hc] = S.mm.n
            w0 = tt * 512 - 2
            lo = 2 if tt == 0 else 0
            hi = 2 if flush else 512
            P.wait("vector", S.mm)
            for m in range(DC):
                P.wait("sync", S.ev)
                if flush:
                    P.op("vector", lambda e, m=m, stl_t=stl_t: e.memset(
                        stl_t[:, m * 512:(m + 1) * 512], 0.0), S.ev, 1)
                    P.wait("sync", S.ev, S.ev.n)
                    P.op("sync", lambda e, m=m, w0=w0, stl_t=stl_t: e.dma_start(
                        out=stl_t[:, m * 512:m * 512 + 2],
                        in_=sin[m * 128:(m + 1) * 128, w0:w0 + 2]), S.ldx, 16)
                elif tt == 0:
                    P.op("vector", lambda e, m=m, stl_t=stl_t: e.memset(
                        stl_t[:, m * 512:m * 512 + 2], 0.0), S.ev, 1)
                    P.wait("sync", S.ev, S.ev.n)
                    P.op("sync", lambda e, m=m, stl_t=stl_t: e.dma_start(
                        out=stl_t[:, m * 512 + 2:(m + 1) * 512],
                        in_=sin[m * 128:(m + 1) * 128, 0:510]), S.ldx, 16)
                else:
                    P.op("sync", lambda e, m=m, w0=w0, stl_t=stl_t: e.dma_start(
                        out=stl_t[:, m * 512:(m + 1) * 512],
                        in_=sin[m * 128:(m + 1) * 128, w0:w0 + 512]), S.ldx, 16)
                P.wait("vector", S.ldx)
                P.op("vector", lambda e, m=m: e.scalar_tensor_tensor(
                    sq2[:, 0:512], ps[m][:, 0:512], qsc[:, 1536:1537],
                    qsc[:, 512:1024], ALU.mult, ALU.mult), None)
                P.op("vector", lambda e, m=m, stl_t=stl_t: e.tensor_tensor(
                    stl_t[:, m * 512:(m + 1) * 512], sq2[:, 0:512],
                    stl_t[:, m * 512:(m + 1) * 512], ALU.add), S.ev, 1)
            self.layer_norm(stl_t, sq2, 512, affine=False, dram=sout,
                            dram_col0=w0, store_lo=lo, store_hi=hi)
            lnv[tt] = S.ev.n
        P.wait("sync", S.st)

    def quantize(self, src, width, nch, dst_bf, qsc, mxs, scr1, scr2, sw_col,
                 out_slot_w):
        dd, S, T, P, C = self.dd, self.S, self.T, self.P, self.C
        cst, ps = T["cst"], T["ps"]
        P.op("vector", lambda e: e.scalar_tensor_tensor(
            mxs[:, 0:width], src[:, 0:width], -1.0, src[:, 0:width],
            ALU.mult, ALU.max), None)
        for m in range(1, nch):
            P.op("vector", lambda e, m=m: e.tensor_tensor(
                mxs[:, 0:width], mxs[:, 0:width],
                src[:, m * width:(m + 1) * width], ALU.max), None)
            P.op("vector", lambda e, m=m: e.scalar_tensor_tensor(
                mxs[:, 0:width], src[:, m * width:(m + 1) * width], -1.0,
                mxs[:, 0:width], ALU.mult, ALU.max), S.ev if m == nch - 1 else None, 1)
        self.qscales(mxs, qsc, sw_col)
        for m in range(nch):
            P.wait("vector", S.mm)
            P.op("vector", lambda e, m=m: e.tensor_tensor(
                scr1[:, 0:width], src[:, m * width:(m + 1) * width],
                qsc[:, 0:width], ALU.mult), None)
            P.op("vector", lambda e: e.tensor_scalar(
                scr2[:, 0:width], scr1[:, 0:width], 0.5, None, ALU.add,
                ALU.bypass), None)
            P.op("vector", lambda e: e.tensor_copy(
                mxs[:, 0:width].bitcast(I32)[:, 0:width], scr2[:, 0:width]), None)
            P.op("vector", lambda e: e.tensor_tensor(
                scr1[:, 0:width], mxs[:, 0:width].bitcast(I32)[:, 0:width],
                scr2[:, 0:width], ALU.is_gt), None)
            P.op("vector", lambda e: e.tensor_tensor(
                scr2[:, 0:width], mxs[:, 0:width].bitcast(I32)[:, 0:width],
                scr1[:, 0:width], ALU.subtract), None)
            P.op("vector", lambda e, m=m: e.tensor_scalar(
                dst_bf[:, m * out_slot_w:m * out_slot_w + width],
                scr2[:, 0:width], -128.0, 127.0, ALU.max, ALU.min), S.ev, 1)

    def qscales(self, mxs, qsc, sw_col):
        dd, S, T, P, C = self.dd, self.S, self.T, self.P, self.C
        cst, sm, ps = T["cst"], T["smls"], T["ps"]
        P.wait("tensor", S.ev)
        P.wait("tensor", S.aux)
        for tc in range(4):
            P.op("tensor", lambda e, tc=tc: e.transpose(
                ps[6][:, tc * 128:(tc + 1) * 128],
                mxs[:, tc * 128:(tc + 1) * 128], T["ident"][:]), S.mm, 1)
        P.wait("vector", S.mm)
        for tc in range(4):
            P.op("vector", lambda e, tc=tc: e.tensor_reduce(
                sm[:, 560 + tc:560 + tc + 1], ps[6][:, tc * 128:(tc + 1) * 128],
                AXX, ALU.max), S.vx, 1)
        P.wait("vector", S.vx)
        P.op("vector", lambda e: e.nop(), S.ev, 1)
        mc = sm[:, 566:570]
        sx = sm[:, 570:574]
        iv = sm[:, 574:578]
        self.vop(lambda e: e.tensor_scalar(
            mc, sm[:, 560:564], 1e-5, None, ALU.max, ALU.bypass))
        self.vop(lambda e: e.reciprocal(sx, mc))
        self.vop(lambda e: e.tensor_scalar(
            sx, sx, 127.0, None, ALU.mult, ALU.bypass))
        self.vop(lambda e: e.tensor_scalar(
            iv, mc, 1.0 / 127.0, None, ALU.mult, ALU.bypass))
        self.vop(lambda e: e.tensor_copy(sm[:, 580:584], sx))
        P.op("vector", lambda e: e.tensor_copy(sm[:, 584:588], iv), S.ev, 1)
        P.wait("tensor", S.ev)
        P.op("tensor", lambda e: e.transpose(ps[7][:, 0:128],
                                             sm[:, 580:708], T["ident"][:]), S.mm, 1)
        P.wait("vector", S.mm)
        P.op("vector", lambda e: e.tensor_copy(
            sm[0:8, 2176:2304], ps[7][0:8, 0:128]), S.ev, 1)
        P.wait("sync", S.ev)
        P.op("sync", lambda e: e.dma_start(
            out=sm[0:1, 280:280 + 1024], in_=sm[0:8, 2176:2304]), S.ld2, 16)
        sxr = sm[0:1, 280:280 + 512]
        ivr = sm[0:1, 280 + 512:280 + 1024]
        P.wait("tensor", S.ld2)
        P.op("tensor", lambda e: e.matmul(ps[6][:, 0:512],
                                          cst[0:1, C["onesr"]:C["onesr"] + 128],
                                          sxr, start=True, stop=True), None)
        P.op("tensor", lambda e: e.matmul(ps[7][:, 0:512],
                                          cst[0:1, C["onesr"]:C["onesr"] + 128],
                                          ivr, start=True, stop=True), S.mm, 1)
        P.wait("vector", S.mm)
        P.op("vector", lambda e: e.tensor_copy(qsc[:, 0:512], ps[6][:, 0:512]), None)
        P.op("vector", lambda e: e.tensor_copy(qsc[:, 512:1024], ps[7][:, 0:512]),
             None)
        P.op("vector", lambda e, sw_col=sw_col: e.tensor_copy(
            qsc[:, 1536:1537], cst[:, sw_col:sw_col + 1]), S.ev, 1)
        P.op("tensor", lambda e: e.nop(), S.aux, 1)

    def phaseCD(self):
        dd, S, T, P, C = self.dd, self.S, self.T, self.P, self.C
        cst, sm = T["cst"], T["smls"]
        pstate = T["pstate"]
        pv = [pstate[:, i * 512:(i + 1) * 512] for i in range(8)]
        xpw, xsw, xsb, dtw = T["xpw"], T["xsw"], T["xsb"], T["dtw"]
        abf, zbf, u1b, u2b, ubf = T["abf"], T["zbf"], T["u1b"], T["u2b"], T["ubf"]
        ywn, wsc, stlc, zst = T["ywn"], T["wsc"], T["stlc"], T["zst"]
        sow, zwin = T["sow"], T["zwin"]
        msc, ysm = T["msc"], T["ysm"]
        sB = dd["sB"]
        NW = L // 512
        nblk = 512 // KB
        P.op("vector", lambda e: e.memset(
            pstate[:, 2048 + (KB - 1) * 256:4096], 0.0), S.ev, 1)
        for w in range(NW):
            P.wait("sync", S.ev)
            for m in range(DC):
                P.op("sync", lambda e, m=m, w=w: e.dma_start(
                    out=stlc[:, m * 512:(m + 1) * 512],
                    in_=sB[m * 128:(m + 1) * 128, w * 512:(w + 1) * 512]), S.ld, 16)
            P.wait("vector", S.ld)
            for m in range(DC):
                P.op("vector", lambda e, m=m: e.tensor_copy(
                    xsb[:][:, m * 512:(m + 1) * 512],
                    stlc[:, m * 512:(m + 1) * 512]), S.ev, 1)
            P.wait("vector", S.evs, S.evs.n)
            mmw = {}
            evw = {}
            for fc in range(16):
                par = fc % 2
                wsl = wsc[:, par * 1024:(par + 1) * 1024]
                if fc >= 2:
                    P.wait("sync", S.mm, mmw[fc - 2])
                else:
                    P.wait("sync", S.mm, S.mm.n)
                P.op("sync", lambda e, fc=fc, wsl=wsl: e.dma_start(
                    out=wsl, in_=rap(dd["w1in_b"], fc * 128,
                                     [[2 * D, 128], [128 * 2 * D, DC], [1, 128]])),
                    S.ldw, 16)
                P.wait("tensor", S.ldw)
                if fc >= 2:
                    P.wait("tensor", S.ev, evw[fc - 2])
                else:
                    P.wait("tensor", S.ev)
                bank = fc % 2
                for kc in range(DC):
                    P.op("tensor", lambda e, fc=fc, kc=kc, wsl=wsl, bank=bank: e.matmul(
                        pv[bank], wsl[:, kc * 128:(kc + 1) * 128],
                        xsb[:][:, kc * 512:(kc + 1) * 512],
                        start=(kc == 0), stop=(kc == DC - 1)),
                        S.mm if kc == DC - 1 else None, 1)
                mmw[fc] = S.mm.n
                P.wait("vector", S.mm)
                if fc < DC:
                    P.op("vector", lambda e, fc=fc, bank=bank: e.tensor_scalar(
                        xpw[:, fc * 515 + 3:fc * 515 + 515], pv[bank],
                        cst[:, C["b_in"] + fc:C["b_in"] + fc + 1], None,
                        ALU.add, ALU.bypass), S.ev, 1)
                else:
                    zc = fc - DC
                    zsl = zwin[:, zc * 512:(zc + 1) * 512]
                    P.op("vector", lambda e, zc=zc, bank=bank, zsl=zsl: e.tensor_scalar(
                        zsl, pv[bank],
                        cst[:, C["b_in"] + DC + zc:C["b_in"] + DC + zc + 1], None,
                        ALU.add, ALU.bypass), S.ev, 1)
                    if "zgate" in dd.get("_dumps", ()):
                        P.wait("sync", S.ev)
                        P.op("sync", lambda e, zc=zc, w=w, zsl=zsl: e.dma_start(
                            out=dd["zgate"][zc * 128:(zc + 1) * 128,
                                            w * 512:(w + 1) * 512],
                            in_=zsl), S.st, 16)
                evw[fc] = S.ev.n
            if "xpd" in dd.get("_dumps", ()):
                P.wait("sync", S.ev)
                for m in range(DC):
                    P.op("sync", lambda e, m=m, w=w: e.dma_start(
                        out=dd["xpd"][m * 128:(m + 1) * 128, w * 512:(w + 1) * 512],
                        in_=xpw[:, m * 515 + 3:(m + 1) * 515]), S.st, 16)
                P.wait("sync", S.st)
                P.wait("vector", S.st)
            if w == 0:
                P.op("vector", lambda e: e.memset(
                    rap(xpw[:], 0, [[DC * 515, 128], [515, DC], [1, 3]]), 0.0), S.ev, 1)
            silc = {}
            for m in range(DC):
                xc = xpw[:, m * 515:(m + 1) * 515]
                base = (m % 2) * 1024
                if m >= 2:
                    P.wait("vector", S.evs, silc[m - 2])
                P.op("vector", lambda e, xc=xc, m=m, base=base: e.tensor_scalar(
                    ysm[:, base:base + 512], xc[:, 0:512],
                    cst[:, C["cw4"] + m * 4:C["cw4"] + m * 4 + 1], None,
                    ALU.mult, ALU.bypass), None)
                for k in range(1, 4):
                    srcb = ysm[:, base:base + 512] if k % 2 == 1 else \
                        ysm[:, base + 512:base + 1024]
                    dstb = ysm[:, base + 512:base + 1024] if k % 2 == 1 else \
                        ysm[:, base:base + 512]
                    P.op("vector", lambda e, xc=xc, m=m, k=k, srcb=srcb, dstb=dstb:
                         e.scalar_tensor_tensor(
                             dstb, xc[:, k:k + 512],
                             cst[:, C["cw4"] + m * 4 + k:C["cw4"] + m * 4 + k + 1],
                             srcb, ALU.mult, ALU.add),
                         S.ev if k == 3 else None, 1)
                P.wait("scalar", S.ev)
                P.op("scalar", lambda e, m=m, base=base: e.activation(
                    xsw[:, m * 512:(m + 1) * 512], ysm[:, base + 512:base + 1024],
                    ACT.Silu,
                    bias=cst[:, C["cb4"] + m:C["cb4"] + m + 1], scale=1.0),
                    S.evs, 1)
                silc[m] = S.evs.n
                P.op("vector", lambda e, xc=xc, m=m: e.tensor_copy(
                    msc[:, m * 3:m * 3 + 3], xc[:, 512:515]), S.ev, 1)
            P.wait("vector", S.evs)
            for m in range(DC):
                P.op("vector", lambda e, m=m: e.tensor_copy(
                    xpw[:, m * 515:m * 515 + 3], msc[:, m * 3:m * 3 + 3]), S.vx, 1)
            P.wait("vector", S.vx)
            P.op("vector", lambda e: e.nop(), S.ev, 1)
            for m in range(DC):
                P.op("vector", lambda e, m=m: e.tensor_copy(
                    xsb[:][:, m * 512:(m + 1) * 512],
                    xsw[:, m * 512:(m + 1) * 512]), S.ev, 1)
            P.wait("vector", S.gp, S.gp.n)
            mmd = {}
            exc = {}
            clc = {}
            for fc in range(DC):
                par = fc % 2
                wsl = wsc[:, par * 2048:par * 2048 + 1024]
                base = par * 1024
                if fc >= 2:
                    P.wait("sync", S.mm, mmd[fc - 2])
                elif fc == 0:
                    P.wait("sync", S.mm, mmw[14])
                P.op("sync", lambda e, fc=fc, wsl=wsl: e.dma_start(
                    out=wsl[:, 0:1024],
                    in_=rap(dd["wdt_b"], fc * 128,
                            [[D, 128], [128 * D, DC], [1, 128]])), S.ldw, 16)
                P.wait("tensor", S.ldw)
                if fc >= 2:
                    P.wait("tensor", S.evs, exc[fc - 2])
                else:
                    P.wait("tensor", S.evs, S.evs.n)
                bank = 2 + fc % 2
                for kc in range(DC):
                    P.op("tensor", lambda e, fc=fc, kc=kc, wsl=wsl, bank=bank: e.matmul(
                        pv[bank],
                        wsl[:, kc * 128:(kc + 1) * 128],
                        xsb[:][:, kc * 512:(kc + 1) * 512],
                        start=(kc == 0), stop=(kc == DC - 1)),
                        S.mm if kc == DC - 1 else None, 1)
                mmd[fc] = S.mm.n
                P.wait("scalar", S.mm, mmd[fc])
                if fc >= 2:
                    P.wait("scalar", S.ev, clc[fc - 2])
                P.op("scalar", lambda e, fc=fc, bank=bank, base=base: e.activation(
                    ysm[:, base:base + 512], pv[bank], ACT.Exp,
                    bias=cst[:, C["b_dt"] + fc:C["b_dt"] + fc + 1], scale=1.0),
                    S.evs, 1)
                exc[fc] = S.evs.n
                P.op("scalar", lambda e, base=base: e.activation(
                    ysm[:, base + 512:base + 1024], ysm[:, base:base + 512],
                    ACT.Ln, bias=1.0, scale=1.0), S.evs, 1)
                P.wait("vector", S.evs)
                P.op("vector", lambda e, fc=fc, base=base: e.tensor_scalar(
                    dtw[:, fc * 512:(fc + 1) * 512], ysm[:, base + 512:base + 1024],
                    1e-3, 0.2, ALU.max, ALU.min), S.ev, 1)
                clc[fc] = S.ev.n
            if "xsd" in dd.get("_dumps", ()):
                P.wait("sync", S.ev)
                for m in range(DC):
                    P.op("sync", lambda e, m=m, w=w: e.dma_start(
                        out=dd["xsd"][m * 128:(m + 1) * 128, w * 512:(w + 1) * 512],
                        in_=xsw[:, m * 512:(m + 1) * 512]), S.st, 16)
            if "dtd" in dd.get("_dumps", ()):
                P.wait("sync", S.ev)
                for m in range(DC):
                    P.op("sync", lambda e, m=m, w=w: e.dma_start(
                        out=dd["dtd"][m * 128:(m + 1) * 128, w * 512:(w + 1) * 512],
                        in_=dtw[:, m * 512:(m + 1) * 512]), S.st, 16)
                P.wait("sync", S.st)
                P.wait("vector", S.st)
            def emit_qr(b):
                slot = b % 2
                t0 = b * KB
                qb = sm[:, slot * 128:slot * 128 + 64]
                rb = sm[:, slot * 128 + 64:slot * 128 + 128]
                dtap = rap(dtw[:], t0, [[DC * 512, 128], [1, KB], [512, DC]])
                xsap = rap(xsw[:], t0, [[DC * 512, 128], [1, KB], [512, DC]])
                P.op("vector", lambda e, qb=qb, dtap=dtap, xsap=xsap:
                     e.tensor_tensor(qb, dtap, xsap, ALU.mult), None)
                P.op("vector", lambda e, qb=qb, rb=rb, xsap=xsap:
                     e.tensor_tensor(rb, qb, xsap, ALU.mult), S.vx, 1)
                self._qr_vx[b] = S.vx.n
            self._qr_vx = getattr(self, "_qr_vx", {})
            self._st_sc = getattr(self, "_st_sc", {})
            self._gp_z = getattr(self, "_gp_z", {})
            self._gp_all = getattr(self, "_gp_all", {})
            self._ex_evs = getattr(self, "_ex_evs", {})
            self._gp_prev_all = getattr(self, "_gp_prev_all", 0)
            P.wait("vector", S.gp, S.gp.n)
            emit_qr(0)
            for b in range(nblk):
                par = b % 2
                t0 = b * KB
                zb = zbf[:, par * 2048:(par + 1) * 2048]
                ab = abf[:, par * 2048:(par + 1) * 2048]
                ub = ubf[:, 0:2048]
                sh = pstate[:, par * 2048:(par + 1) * 2048]
                shp = pstate[:, (1 - par) * 2048:(2 - par) * 2048]
                slot = b % 2
                if b >= 2:
                    P.wait("vector", S.evs, self._ex_evs[b - 2])
                zap = rap(cst[:], C["A"], [[1920, 128], [0, KB], [32, DC], [1, 32]])
                dtbc = rap(dtw[:], t0, [[DC * 512, 128], [1, KB], [512, DC], [0, 32]])
                P.op("vector", lambda e, zb=zb, dtbc=dtbc, zap=zap: e.tensor_tensor(
                    zb, dtbc, zap, ALU.mult), S.vx, 1)
                zvx = S.vx.n
                P.wait("scalar", S.vx, zvx)
                if b >= 2:
                    P.wait("scalar", S.sc, self._st_sc[b - 2])
                P.op("scalar", lambda e, ab=ab, zb=zb: e.activation(
                    ab, zb, ACT.Exp, scale=1.0), S.evs, 1)
                self._ex_evs[b] = S.evs.n
                wBap = rap(cst[:], C["wB"], [[1920, 128], [0, KB], [32, DC], [1, 32]])
                bBap = rap(cst[:], C["bB"], [[1920, 128], [0, KB], [32, DC], [1, 32]])
                rbc = rap(sm[:], slot * 128 + 64, [[2304, 128], [8, KB], [1, DC], [0, 32]])
                qbc = rap(sm[:], slot * 128, [[2304, 128], [8, KB], [1, DC], [0, 32]])
                P.op("vector", lambda e, rbc=rbc, wBap=wBap: e.tensor_tensor(
                    u1b[:, 0:2048], rbc, wBap, ALU.mult), None)
                P.op("vector", lambda e, qbc=qbc, bBap=bBap: e.tensor_tensor(
                    u2b[:, 0:2048], qbc, bBap, ALU.mult), None)
                P.op("vector", lambda e, ub=ub: e.tensor_tensor(
                    ub, u1b[:, 0:2048], u2b[:, 0:2048], ALU.add), None)
                if b + 1 < nblk:
                    emit_qr(b + 1)
                P.wait("vector", S.evs, self._ex_evs[b])
                for k in range(KB):
                    sprev = shp[:, (KB - 1) * 256:KB * 256] if k == 0 else \
                        sh[:, (k - 1) * 256:k * 256]
                    scur = sh[:, k * 256:(k + 1) * 256]
                    mtl = ysm[:, 0:256]
                    P.op("vector", lambda e, mtl=mtl, ab=ab, k=k, sprev=sprev:
                         e.tensor_tensor(mtl, ab[:, k * 256:(k + 1) * 256],
                                         sprev, ALU.mult), None)
                    P.op("vector", lambda e, mtl=mtl, ub=ub, k=k: e.tensor_tensor(
                        mtl, mtl, ub[:, k * 256:(k + 1) * 256], ALU.add), None)
                    P.op("vector", lambda e, scur=scur, mtl=mtl: e.tensor_scalar(
                        scur, mtl, -10.0, 10.0, ALU.max, ALU.min),
                        S.sc if k == KB - 1 else None, 1)
                self._st_sc[b] = S.sc.n
                wCap = rap(cst[:], C["wC"], [[1920, 128], [0, KB], [32, DC], [1, 32]])
                bCap = rap(cst[:], C["bC"], [[1920, 128], [0, KB], [32, DC], [1, 32]])
                shap = rap(pstate[:], par * 2048, [[4096, 128], [256, KB], [32, DC], [1, 32]])
                xsap = rap(xsw[:], t0, [[DC * 512, 128], [1, KB], [512, DC]])
                P.op("vector", lambda e, wCap=wCap, shap=shap: e.tensor_tensor(
                    ysm[:, 0:2048], shap, wCap, ALU.mult), None)
                P.op("vector", lambda e: e.tensor_reduce(
                    sm[:, 256:320], rap(ysm[:], 0, [[2048, 128], [32, 64], [1, 32]]),
                    AXX, ALU.add), None)
                P.op("vector", lambda e, bCap=bCap, shap=shap: e.tensor_tensor(
                    ysm[:, 0:2048], shap, bCap, ALU.mult), None)
                P.op("vector", lambda e: e.tensor_reduce(
                    sm[:, 320:384], rap(ysm[:], 0, [[2048, 128], [32, 64], [1, 32]]),
                    AXX, ALU.add), None)
                ywap = rap(ywn[:], t0, [[DC * 512, 128], [1, KB], [512, DC]])
                P.op("vector", lambda e, xsap=xsap: e.tensor_tensor(
                    sm[:, 256:320], sm[:, 256:320], xsap, ALU.mult), None)
                P.op("vector", lambda e, ywap=ywap: e.tensor_tensor(
                    ywap, sm[:, 256:320], sm[:, 320:384], ALU.add), S.sc, 1)
            if "ybuf" in dd.get("_dumps", ()):
                P.wait("sync", S.sc)
                for m in range(DC):
                    P.op("sync", lambda e, m=m, w=w: e.dma_start(
                        out=dd["ybuf"][m * 128:(m + 1) * 128,
                                       w * 512:(w + 1) * 512],
                        in_=ywn[:, m * 512:(m + 1) * 512]), S.st, 16)
                P.wait("vector", S.st)
            ysc = S.sc.n
            silev = {}
            sowev = {}
            woutmm = {}
            evo = {}
            for m in range(DC):
                gsl = ysm[:, (m % 2) * 512:(m % 2) * 512 + 512]
                if m < 2:
                    P.wait("scalar", S.sc, ysc)
                else:
                    P.wait("scalar", S.ev, sowev[m - 2])
                P.op("scalar", lambda e, m=m, gsl=gsl: e.activation(
                    gsl, zwin[:, m * 512:(m + 1) * 512], ACT.Silu, scale=1.0),
                    S.evs, 1)
                silev[m] = S.evs.n
                P.wait("vector", S.evs, silev[m])
                P.op("vector", lambda e, m=m, gsl=gsl: e.tensor_tensor(
                    sow[:, m * 512:(m + 1) * 512], ywn[:, m * 512:(m + 1) * 512],
                    gsl, ALU.mult), S.ev, 1)
                sowev[m] = S.ev.n
            wout_pre = S.mm.n
            for fc in range(DC):
                wsl = wsc[:, 0:1024] if fc % 2 == 0 else wsc[:, 2048:3072]
                P.wait("sync", S.mm, woutmm[fc - 2] if fc >= 2 else wout_pre)
                P.op("sync", lambda e, fc=fc, wsl=wsl: e.dma_start(
                    out=wsl, in_=rap(dd["w1out_b"], fc * 128,
                                     [[D, 128], [128 * D, DC], [1, 128]])),
                    S.ldw, 16)
                P.wait("tensor", S.ldw)
                if fc == 0:
                    P.wait("tensor", S.ev, sowev[DC - 1])
                if fc >= 4:
                    P.wait("tensor", S.ev, evo[fc - 4])
                for kc in range(DC):
                    P.op("tensor", lambda e, fc=fc, kc=kc, wsl=wsl: e.matmul(
                        pv[fc % 4], wsl[:, kc * 128:(kc + 1) * 128],
                        sow[:, kc * 512:(kc + 1) * 512],
                        start=(kc == 0), stop=(kc == DC - 1)),
                        S.mm if kc == DC - 1 else None, 1)
                woutmm[fc] = S.mm.n
                P.wait("vector", S.mm, woutmm[fc])
                P.op("vector", lambda e, fc=fc: e.tensor_scalar(
                    ysm[:, 1024:1536], pv[fc % 4],
                    cst[:, C["b_out"] + fc:C["b_out"] + fc + 1], None,
                    ALU.add, ALU.bypass), None)
                P.op("vector", lambda e, fc=fc: e.tensor_tensor(
                    xpw[:, fc * 515 + 3:fc * 515 + 515], ysm[:, 1024:1536],
                    stlc[:, fc * 512:(fc + 1) * 512], ALU.add), S.ev, 1)
                evo[fc] = S.ev.n
            P.wait("tensor", S.ev, S.ev.n)
            self.layer_norm(xpw[:], ysm[:, 1024:1536], 512, affine=True,
                            dst_sb=xpw[:], pstats=(pv[0], pv[1], 4096, 0),
                            cstride=515, coff=3)
            P.wait("vector", S.ev)
            for m in range(DC):
                P.op("vector", lambda e, m=m: e.tensor_tensor(
                    xpw[:, m * 515 + 3:(m + 1) * 515],
                    xpw[:, m * 515 + 3:(m + 1) * 515],
                    stlc[:, m * 512:(m + 1) * 512], ALU.add), S.ev, 1)
            P.wait("tensor", S.ev, S.ev.n)
            self.layer_norm(xpw[:], ysm[:, 1024:1536], 512, affine=False,
                            dram=dd["sC"], dram_col0=w * 512,
                            pstats=(pv[0], pv[1], 4096, 0), cstride=515, coff=3)
        P.wait("sync", S.st)

    def phaseEpost(self):
        dd, S, T, P, C = self.dd, self.S, self.T, self.P, self.C
        cst = T["cst"]
        ywin, gwin, sow, resw = T["ywin"], T["gwin"], T["sow"], T["resw"]
        pl2, wso, sq3, zld = T["pl2"], T["wso"], T["sq3"], T["zld"]
        ps = T["ps"]
        for tt in range(NTT):
            P.wait("sync", S.ev)
            for m in range(DC):
                P.op("sync", lambda e, m=m, tt=tt: e.dma_start(
                    out=ywin[:, m * 512:(m + 1) * 512],
                    in_=dd["ybuf"][m * 128:(m + 1) * 128,
                                   tt * 512:(tt + 1) * 512]), S.ld, 16)
                P.wait("sync", S.evs)
                P.op("sync", lambda e, m=m, tt=tt: e.dma_start(
                    out=zld[:, 0:512],
                    in_=dd["zgate"][m * 128:(m + 1) * 128,
                                    tt * 512:(tt + 1) * 512]), S.ldx, 16)
                P.wait("scalar", S.ldx)
                P.wait("scalar", S.ev)
                P.op("scalar", lambda e: e.activation(
                    gwin[:, 0:512], zld[:, 0:512],
                    ACT.Silu, scale=1.0), S.evs, 1)
                P.wait("vector", S.evs)
                P.wait("vector", S.ld)
                P.op("vector", lambda e, m=m: e.tensor_tensor(
                    sq3[:, 0:512], ywin[:, m * 512:(m + 1) * 512], gwin[:, 0:512],
                    ALU.mult), None)
                P.op("vector", lambda e, m=m: e.tensor_copy(
                    sow[:, m * 512:(m + 1) * 512], sq3[:, 0:512]),
                    S.ev, 1)
            for fc in range(DC):
                par = fc % 2
                wsl = wso[:, par * 1024:(par + 1) * 1024]
                if fc >= 2:
                    P.wait("sync", S.mm)
                P.op("sync", lambda e, fc=fc, wsl=wsl: e.dma_start(
                    out=wsl, in_=rap(dd["w1out_b"], fc * 128,
                                     [[D, 128], [128 * D, DC], [1, 128]])),
                    S.ldw, 16)
                P.wait("tensor", S.ldw)
                P.wait("tensor", S.ev)
                for kc in range(DC):
                    P.op("tensor", lambda e, fc=fc, kc=kc, wsl=wsl: e.matmul(
                        ps[fc % 4][:, 0:512], wsl[:, kc * 128:(kc + 1) * 128],
                        sow[:, kc * 512:(kc + 1) * 512],
                        start=(kc == 0), stop=(kc == DC - 1)),
                        S.mm if kc == DC - 1 else None, 1)
                P.wait("sync", S.ev)
                P.op("sync", lambda e, fc=fc, tt=tt: e.dma_start(
                    out=resw[:, fc * 512:(fc + 1) * 512],
                    in_=dd["sB"][fc * 128:(fc + 1) * 128,
                                 tt * 512:(tt + 1) * 512]), S.ldx, 16)
                P.wait("vector", S.mm)
                P.wait("vector", S.ldx)
                P.op("vector", lambda e, fc=fc: e.tensor_scalar(
                    sq3[:, 0:512], ps[fc % 4][:, 0:512],
                    cst[:, C["b_out"] + fc:C["b_out"] + fc + 1], None,
                    ALU.add, ALU.bypass), None)
                P.op("vector", lambda e, fc=fc: e.tensor_tensor(
                    pl2[:, fc * 512:(fc + 1) * 512], sq3[:, 0:512],
                    resw[:, fc * 512:(fc + 1) * 512], ALU.add), S.ev, 1)
            self.layer_norm(pl2, sq3, 512, affine=True, dst_sb=pl2)
            P.wait("vector", S.ev)
            for m in range(DC):
                P.op("vector", lambda e, m=m: e.tensor_tensor(
                    pl2[:, m * 512:(m + 1) * 512], pl2[:, m * 512:(m + 1) * 512],
                    resw[:, m * 512:(m + 1) * 512], ALU.add), S.ev, 1)
            self.layer_norm(pl2, sq3, 512, affine=False, dram=dd["sC"],
                            dram_col0=tt * 512)
        P.wait("sync", S.st)


CORE = {}


def kernel(**inputs):
    g = {k: np.asarray(v) for k, v in inputs.items()}
    x = g["L0_attn_x"].astype(np.float32, copy=False)
    xfull = np.ascontiguousarray(x.transpose(2, 0, 1).reshape(D, B * L))
    W_in = g["L0_attn_W_in"]

    def colmaj(v, nch):
        return np.ascontiguousarray(v.reshape(nch, 128).T)

    def scarr(v):
        return np.ascontiguousarray(v.reshape(8, 128, 32).transpose(1, 0, 2)
                                    .reshape(128, 256))

    common = dict(
        xfull=xfull,
        w0out=np.ascontiguousarray(g["L0_attn_W_out"]),
        f_w1=np.ascontiguousarray(g["L0_attn_f_w1"][:, :, 0]),
        f_b1T=np.ascontiguousarray(g["L0_attn_f_b1"].T),
        w1in=np.ascontiguousarray(g["L1_attn_W_in"]),
        b_in=colmaj(g["L1_attn_b_in"], 16),
        cw4=np.ascontiguousarray(g["L1_attn_conv_w"][:, 0, :].reshape(8, 128, 4)
                                 .transpose(1, 0, 2).reshape(128, 32)),
        cb4=colmaj(g["L1_attn_conv_b"], 8),
        A_sc=scarr(g["L1_attn_A_log"]),
        wB_sc=scarr(g["L1_attn_wB"]),
        bB_sc=scarr(g["L1_attn_bB"]),
        wC_sc=scarr(g["L1_attn_wC"]),
        bC_sc=scarr(g["L1_attn_bC"]),
        wdt=np.ascontiguousarray(g["L1_attn_W_dt"]),
        b_dt=colmaj(g["L1_attn_b_dt"], 8),
        w1outw=np.ascontiguousarray(g["L1_attn_W_out"]),
        b_out=colmaj(g["L1_attn_b_out"], 8),
        ln_g=colmaj(g["L1_attn_ln_g"], 8),
        ln_b=colmaj(g["L1_attn_ln_b"], 8),
    )
    for l, pre in ((0, "L0_ffn"), (1, "L1_ffn")):
        common[f"ffn{l}_w1"] = np.ascontiguousarray(g[f"{pre}_w1"])
        common[f"ffn{l}_cw"] = np.ascontiguousarray(
            g[f"{pre}_conv_w"][:, 0, :].reshape(32, 128, 3).transpose(1, 0, 2)
            .reshape(128, 96))
        common[f"ffn{l}_cb"] = colmaj(g[f"{pre}_conv_b"], 32)
        common[f"ffn{l}_w2"] = np.ascontiguousarray(g[f"{pre}_w2"])

    in_maps = []
    for c in range(8):
        m = dict(common)
        m["w0in_s"] = np.ascontiguousarray(np.concatenate(
            [W_in[c * 128:(c + 1) * 128],
             W_in[D + c * 128:D + (c + 1) * 128],
             W_in[2 * D + c * 128:2 * D + (c + 1) * 128]], axis=0))
        m["f_w2s"] = np.ascontiguousarray(
            g["L0_attn_f_w2"][:, c * 128:(c + 1) * 128, :]
            .transpose(1, 0, 2).reshape(128, 128))
        m["f_b2s"] = np.ascontiguousarray(
            g["L0_attn_f_b2"][:, c * 128:(c + 1) * 128].T)
        m["xmine"] = np.ascontiguousarray(x[c].T)
        in_maps.append(m)

    nc = build(dumps=CORE.get("dumps", ()))
    res = run_bass_kernel_spmd(nc, in_maps, core_ids=list(range(8)))
    out = np.stack([np.asarray(res.results[c]["out"]).T for c in range(8)], 0)
    CORE["res"] = res
    return out.astype(np.float32)
